# version 02 (final)
# baseline (speedup 1.0000x reference)
import numpy as np
from contextlib import ExitStack
import concourse.bass as bass
import concourse.mybir as mybir
from concourse.bass_utils import run_bass_kernel_spmd

F32 = mybir.dt.float32
I32 = mybir.dt.int32
BF16 = mybir.dt.bfloat16
AF = mybir.ActivationFunctionType
ALU = mybir.AluOpType
AXX = mybir.AxisListType.X

D = 2048
T = 2048
H = 16
HD = 128
NB = 32
BS = 64
MLP = 16384
G = 4
SCALE = HD ** -0.5
NEG = -1e30
NQC = T // 128
NDCH = D // 128


def _consts():
    c = {}
    idx = np.arange(T)
    blk_end = (np.arange(NB) + 1) * BS - 1
    cmp_ok = idx[:, None] >= blk_end[None, :]
    def tile_t(a):
        W = a.shape[1]
        return np.ascontiguousarray(a.reshape(NQC, 128, W).transpose(1, 0, 2).reshape(128, NQC * W))
    c["cmpmask"] = tile_t(np.where(cmp_ok, 0.0, NEG).astype(np.float32))
    c["anymask"] = tile_t(cmp_ok.any(-1).astype(np.float32).reshape(T, 1))
    c["anyoff"] = tile_t(((~cmp_ok.any(-1)) * 1e30).astype(np.float32).reshape(T, 1))
    cur = np.zeros((T, NB), dtype=np.float32)
    cur[idx, idx // BS] = 2.0
    c["curblk"] = tile_t(cur)
    c["Eexp"] = (idx[None, :] // BS == np.arange(NB)[:, None]).astype(np.float32)
    c["Ablk"] = tile_t((c["Eexp"].T / BS).astype(np.float32))
    kl = np.arange(128)
    c["diagmask"] = np.where(kl[:, None] > kl[None, :], NEG, 0.0).astype(np.float32)
    dq = kl[None, :] - kl[:, None]
    c["swadiag"] = np.where((dq >= 0) & (dq <= BS), 0.0, NEG).astype(np.float32)
    dq2 = 128 + kl[None, :] - kl[:, None]
    c["swaprev"] = np.where((dq2 >= 0) & (dq2 <= BS), 0.0, NEG).astype(np.float32)
    c["ident"] = np.eye(128, dtype=np.float32)
    c["ones_col"] = np.ones((128, 1), dtype=np.float32)
    c["ones_row"] = np.ones((1, 128), dtype=np.float32)
    half = HD // 2
    inv = 1.0 / (10000.0 ** (np.arange(half, dtype=np.float64) / half))
    th = np.arange(T, dtype=np.float64)[:, None] * inv[None, :]
    c["ropecos"] = np.ascontiguousarray(np.cos(th).astype(np.float32).T)
    c["ropesin"] = np.ascontiguousarray(np.sin(th).astype(np.float32).T)
    c["biases"] = np.tile(np.array([-2.0, -4.0, -10.0, 7.5], dtype=np.float32), (128, 1))
    return c


class Sems:
    def __init__(self, nc, ctx, names):
        self.h = {nm: ctx.enter_context(nc.semaphore(nm)) for nm in names}
        self.n = {nm: 0 for nm in names}

    def inc(self, instr, name, amt=1):
        instr.then_inc(self.h[name], amt)
        self.n[name] += amt
        return self.n[name]

    def val(self, name):
        return self.n[name]

    def wait(self, eng, name, value=None):
        v = self.n[name] if value is None else value
        if v > 0:
            eng.wait_ge(self.h[name], v)


class Builder:
    def __init__(self, dev_stop=None):
        self.dev_stop = dev_stop
        self._dbgq = None
        if isinstance(dev_stop, tuple) and dev_stop[0] == "dbg":
            self._dbgq = dev_stop[1]
        self.nc = bass.Bass("TRN2", target_bir_lowering=False, detect_race_conditions=False)
        self.consts = _consts()

    def _dbg(self, name, ap, rows, cols):
        if self._dbgq is None:
            return
        self.barrier()
        c0 = self._dbg_cur
        self._dbg_map[name] = (rows, c0, cols)
        S = self.S

        def _d(gpsimd, ap=ap, rows=rows, c0=c0, cols=cols):
            S.inc(gpsimd.dma_start(out=self.dbg[0:rows, c0:c0 + cols], in_=ap), "GB", 16)
        self.Gp(_d)
        self.barrier()
        self._dbg_cur = c0 + cols

    def V(self, f):
        self.block.vector(f)

    def T_(self, f):
        self.block.tensor(f)

    def Sc(self, f):
        self.block.scalar(f)

    def Sy(self, f):
        self.block.sync(f)

    def Gp(self, f):
        self.block.gpsimd(f)

    def barrier(self):
        S = self.S
        snap = dict(S.n)
        for emit in (self.V, self.T_, self.Sc, self.Sy, self.Gp):
            def w(e, snap=snap):
                for nm, v in snap.items():
                    if v > 0:
                        e.wait_ge(S.h[nm], v)
            emit(w)

    def build(self):
        nc = self.nc
        dram = {}
        dram["x0"] = nc.dram_tensor("x0", [D, T], F32, kind="ExternalInput").ap()
        for nm, shp in [("aWq", [D, D]), ("aWk", [D, D]), ("aWv", [D, D]),
                        ("aWg", [D, 3 * H]), ("aWo", [D, D]),
                        ("bWq", [D, D]), ("bWk", [D, G * HD]), ("bWv", [D, G * HD])]:
            dram[nm] = nc.dram_tensor(nm, shp, F32, kind="ExternalInput").ap()
        dram["bbq"] = nc.dram_tensor("bbq", [128, H], F32, kind="ExternalInput").ap()
        dram["bbk"] = nc.dram_tensor("bbk", [128, G], F32, kind="ExternalInput").ap()
        dram["bbv"] = nc.dram_tensor("bbv", [128, G], F32, kind="ExternalInput").ap()
        dram["gw"] = nc.dram_tensor("gw", [MLP, D // 2], I32, kind="ExternalInput").ap()
        dram["gs"] = nc.dram_tensor("gs", [MLP, D // 32], F32, kind="ExternalInput").ap()
        dram["uw"] = nc.dram_tensor("uw", [MLP, D // 2], I32, kind="ExternalInput").ap()
        dram["us"] = nc.dram_tensor("us", [MLP, D // 32], F32, kind="ExternalInput").ap()
        dram["dw"] = nc.dram_tensor("dw", [D, MLP // 2], I32, kind="ExternalInput").ap()
        dram["ds"] = nc.dram_tensor("ds", [D, MLP // 32], F32, kind="ExternalInput").ap()
        dram["out"] = nc.dram_tensor("out", [D, T], F32, kind="ExternalOutput").ap()
        dram["xA"] = nc.dram_tensor("xA", [D, T], F32, kind="Internal").ap()
        dram["xB"] = nc.dram_tensor("xB", [D, T], F32, kind="Internal").ap()
        dram["qS"] = nc.dram_tensor("qS", [D, T], BF16, kind="Internal").ap()
        dram["kS"] = nc.dram_tensor("kS", [D, T], BF16, kind="Internal").ap()
        dram["oS"] = nc.dram_tensor("oS", [D, T], BF16, kind="Internal").ap()
        dram["gS"] = nc.dram_tensor("gS", [48, T], F32, kind="Internal").ap()
        dram["gT"] = nc.dram_tensor("gT", [MLP // 128, 16, 128, 128], BF16, kind="Internal").ap()
        dram["uT"] = nc.dram_tensor("uT", [MLP // 128, 16, 128, 128], BF16, kind="Internal").ap()
        dram["dT"] = nc.dram_tensor("dT", [16, MLP // 128, 128, 128], BF16, kind="Internal").ap()
        if self._dbgq is not None:
            self.dbg = nc.dram_tensor("dbg", [128, 24576], F32, kind="ExternalOutput").ap()
            self._dbg_cur = 0
            self._dbg_map = {}
        self.dram = dram
        self.ct = {k: nc.inline_tensor(np.ascontiguousarray(v.astype(np.float32)),
                                       name="c_" + k) for k, v in self.consts.items()}

        with ExitStack() as top:
            self.top = top
            self.S = Sems(nc, top, ["LA", "LB", "LC", "LD", "LE", "SA", "SB", "GA", "GB", "GC", "Tm", "Vm", "Scm"])
            self.block = top.enter_context(nc.Block())
            self._load_consts()
            self.dequant_phase()
            self.barrier()
            if self.dev_stop == 0:
                self._dev_out("x0"); return nc
            self.attention("a", src="x0", dst="xA")
            if self._dbgq is not None:
                self.barrier()
                return nc
            if isinstance(self.dev_stop, tuple) and self.dev_stop[0] == "dump":
                nm = self.dev_stop[1]
                self.barrier()

                def _fd(gpsimd, nm=nm):
                    S = self.S
                    S.inc(gpsimd.dma_start(out=self.dram["out"], in_=self.dram[nm]), "GA", 16)
                    S.wait(gpsimd, "GA")
                self.Gp(_fd)
                return nc
            if self.dev_stop == 1:
                self._dev_out("xA"); return nc
            self.ffn(src="xA", dst="xB")
            if self.dev_stop == 2:
                self._dev_out("xB"); return nc
            self.attention("b", src="xB", dst="xA")
            if self.dev_stop == 3:
                self._dev_out("xA"); return nc
            self.ffn(src="xA", dst="out")
            self.barrier()

            def _fin(sync):
                self.S.wait(sync, "SA")
                self.S.wait(sync, "SB")
            self.Sy(_fin)
        return nc

    def _dev_out(self, src):
        S = self.S
        self.barrier()

        def _f(sync):
            S.inc(sync.dma_start(out=self.dram["out"], in_=self.dram[src]), "SA", 16)
            S.wait(sync, "SA")
        self.Sy(_f)

    def _load_consts(self):
        nc, S = self.nc, self.S
        cs = {}
        E = self.top.enter_context
        cs["biases"] = E(nc.sbuf_tensor("sc_biases", [128, 4], F32))
        cs["ident"] = E(nc.sbuf_tensor("sb_ident", [128, 128], BF16))
        cs["ones_col"] = E(nc.sbuf_tensor("sb_ones_col", [128, 1], BF16))
        cs["ones_row"] = E(nc.sbuf_tensor("sb_ones_row", [1, 128], BF16))
        cs["onesf_col"] = E(nc.sbuf_tensor("sf_ones_col", [128, 1], F32))
        cs["onesf_row"] = E(nc.sbuf_tensor("sf_ones_row", [1, 128], F32))
        self.c_sb = cs
        ct = self.ct

        def _ld(gpsimd):
            S.inc(gpsimd.dma_start(out=cs["biases"][:], in_=ct["biases"].ap()), "GA", 16)
            S.inc(gpsimd.dma_start(out=cs["ident"][:], in_=ct["ident"].ap()), "GA", 16)
            S.inc(gpsimd.dma_start(out=cs["ones_col"][:], in_=ct["ones_col"].ap()), "GA", 16)
            S.inc(gpsimd.dma_start(out=cs["ones_row"][:], in_=ct["ones_row"].ap()), "GA", 16)
        self.Gp(_ld)

        def _ones(vector):
            vector.memset(cs["onesf_col"][:], 1.0)
            S.inc(vector.memset(cs["onesf_row"][:], 1.0), "Vm")
        self.V(_ones)
        self.barrier()

    def dequant_phase(self):
        nc, S = self.nc, self.S
        with ExitStack() as ctx:
            CW = 1024
            PW = CW // 2
            E = ctx.enter_context
            sw = E(nc.sbuf_tensor("dq_w", [128, PW * 2], I32))
            ssc = E(nc.sbuf_tensor("dq_s", [128, (CW // 32) * 2], F32))
            silo = E(nc.sbuf_tensor("dq_lo", [128, PW], I32))
            sihi = E(nc.sbuf_tensor("dq_hi", [128, PW], I32))
            ssgn = E(nc.sbuf_tensor("dq_sgn", [128, CW], BF16))
            sm8 = E(nc.sbuf_tensor("dq_m8", [128, CW], BF16))
            sr4 = E(nc.sbuf_tensor("dq_r4", [128, CW], BF16))
            sr6 = E(nc.sbuf_tensor("dq_r6", [128, CW], BF16))
            sr05 = E(nc.sbuf_tensor("dq_r05", [128, CW], BF16))
            smag = E(nc.sbuf_tensor("dq_mag", [128, CW], BF16))
            sssc = E(nc.sbuf_tensor("dq_ssc", [128, CW], BF16))
            sval = E(nc.sbuf_tensor("dq_val", [128, CW], BF16))
            svalT = E(nc.sbuf_tensor("dq_valT", [128, CW * 2], BF16))
            ps_tr = [E(nc.psum_tensor("dq_ps%d" % i, [128, 512], F32)) for i in range(2)]
            biases = self.c_sb["biases"]
            ident = self.c_sb["ident"]

            def evn(t):
                return t[:].rearrange("p (a two) -> p a two", two=2)[:, :, 0]

            def odd(t):
                return t[:].rearrange("p (a two) -> p a two", two=2)[:, :, 1]

            cfg = [("gw", "gs", "gT", MLP, D), ("uw", "us", "uT", MLP, D),
                   ("dw", "ds", "dT", D, MLP)]
            steps = []
            for wnm, snm, vnm, nrows, ncols in cfg:
                for rc in range(nrows // 128):
                    for cc in range(ncols // CW):
                        steps.append((wnm, snm, vnm, rc, cc))
            vdone, stdone, trdone, evdone = {}, {}, {}, {}

            for i, (wnm, snm, vnm, rc, cc) in enumerate(steps):
                par = i % 2
                wv = sw[:, par * PW:(par + 1) * PW]
                scv = ssc[:, par * (CW // 32):(par + 1) * (CW // 32)]

                lsem = "LA" if par == 0 else "LB"

                def _ld(sync, i=i, wnm=wnm, snm=snm, rc=rc, cc=cc, wv=wv, scv=scv, lsem=lsem):
                    if i >= 2:
                        S.wait(sync, "Vm", vdone[i - 2])
                    S.inc(sync.dma_start(out=wv, in_=self.dram[wnm][rc * 128:(rc + 1) * 128,
                                                                     cc * PW:(cc + 1) * PW]),
                          lsem, 16)
                    S.inc(sync.dma_start(out=scv,
                                         in_=self.dram[snm][rc * 128:(rc + 1) * 128,
                                                            cc * (CW // 32):(cc + 1) * (CW // 32)]),
                          lsem, 16)
                self.Sy(_ld)
                dmac = S.val(lsem)

                def _v1(vector, wv=wv, dmac=dmac, lsem=lsem):
                    S.wait(vector, lsem, dmac)
                    vector.tensor_scalar(silo[:], wv, 15, None, ALU.bitwise_and)
                    S.inc(vector.tensor_scalar(sihi[:], wv, 4, 15,
                                               ALU.logical_shift_right, ALU.bitwise_and), "Vm")
                self.V(_v1)

                def _s1(scalar):
                    S.wait(scalar, "Vm")
                    scalar.activation(evn(ssgn), silo[:], AF.Sign, bias=biases[:, 3:4], scale=-1.0)
                    S.inc(scalar.activation(odd(ssgn), sihi[:], AF.Sign,
                                            bias=biases[:, 3:4], scale=-1.0), "Scm")
                self.Sc(_s1)

                def _v2(vector):
                    S.wait(vector, "Scm")
                    vector.scalar_tensor_tensor(evn(sm8), evn(ssgn), 4.0, silo[:], ALU.mult, ALU.add)
                    S.inc(vector.scalar_tensor_tensor(odd(sm8), odd(ssgn), 4.0, sihi[:],
                                                      ALU.mult, ALU.add), "Vm")
                self.V(_v2)

                def _s2(scalar):
                    S.wait(scalar, "Vm")
                    scalar.activation(sr4[:], sm8[:], AF.Relu, bias=biases[:, 1:2], scale=0.5)
                    S.inc(scalar.activation(sr6[:], sm8[:], AF.Relu, bias=biases[:, 2:3],
                                            scale=1.0), "Scm")
                self.Sc(_s2)

                def _v3(vector, scv=scv, i=i):
                    vector.tensor_scalar(sr05[:], sm8[:], 0.5, -2.0, ALU.mult, ALU.add)
                    S.wait(vector, "Scm")
                    if i >= 1:
                        S.wait(vector, "Tm", trdone[i - 1])
                    vector.tensor_tensor(smag[:], sr05[:], sr4[:], ALU.add)
                    vector.tensor_tensor(smag[:], smag[:], sr6[:], ALU.add)
                    scb = scv[:, :, None].broadcast_to([128, CW // 32, 32])
                    vector.tensor_tensor(sssc[:].rearrange("p (a b) -> p a b", b=32),
                                         ssgn[:].rearrange("p (a b) -> p a b", b=32), scb, ALU.mult)
                    S.inc(vector.tensor_tensor(sval[:], smag[:], sssc[:], ALU.mult), "Vm")
                self.V(_v3)
                vdone[i] = S.val("Vm")

                def _t1(tensor, i=i):
                    S.wait(tensor, "Vm", vdone[i])
                    if i >= 1:
                        S.wait(tensor, "Scm", evdone[i - 1])
                    for j in range(8):
                        m = tensor.matmul(ps_tr[j // 4][:, (j % 4) * 128:(j % 4 + 1) * 128],
                                          sval[:, j * 128:(j + 1) * 128], ident[:],
                                          start=True, stop=True)
                    S.inc(m, "Tm")
                self.T_(_t1)
                trdone[i] = S.val("Tm")

                def _e1(scalar, par=par, i=i):
                    S.wait(scalar, "Tm", trdone[i])
                    if i >= 2:
                        S.wait(scalar, stdone[i - 2][0], stdone[i - 2][1])
                    scalar.copy(svalT[:, par * CW:par * CW + 512], ps_tr[0][:])
                    S.inc(scalar.copy(svalT[:, par * CW + 512:(par + 1) * CW], ps_tr[1][:]), "Scm")
                self.Sc(_e1)
                evdone[i] = S.val("Scm")

                ssem = "GA" if par == 0 else "GB"

                def _st(gpsimd, i=i, par=par, vnm=vnm, rc=rc, cc=cc, ssem=ssem):
                    S.wait(gpsimd, "Scm", evdone[i])
                    vt = svalT[:, par * CW:(par + 1) * CW].rearrange("p (j n) -> p j n", n=128)
                    S.inc(gpsimd.dma_start(out=self.dram[vnm][rc, 8 * cc:8 * cc + 8, :, :].rearrange("j p n -> p j n"),
                                           in_=vt), ssem, 16)
                self.Gp(_st)
                stdone[i] = (ssem, S.val(ssem))

    def _ln_stats(self, Eo, src, tag):
        nc, S = self.nc, self.S
        rbc = Eo(nc.sbuf_tensor("ln_rbc" + tag, [128, T], F32))
        murbc = Eo(nc.sbuf_tensor("ln_mbc" + tag, [128, T], F32))
        with ExitStack() as ctx:
            E = ctx.enter_context
            xa = E(nc.sbuf_tensor("ln_x" + tag, [128, 512 * 2], F32))
            xsq = E(nc.sbuf_tensor("ln_sq" + tag, [128, 512 * 2], F32))
            rows = E(nc.sbuf_tensor("ln_rows" + tag, [1, 3 * T], F32))
            psA = E(nc.psum_tensor("ln_psA" + tag, [1, 512], F32))
            psB = E(nc.psum_tensor("ln_psB" + tag, [1, 512], F32))
            psC = E(nc.psum_tensor("ln_psC" + tag, [128, 512], F32))
            psD = E(nc.psum_tensor("ln_psD" + tag, [128, 512], F32))
            onesf = self.c_sb["onesf_col"]
            src_ap = self.dram[src]
            vdone, tdone, scdone = {}, {}, {}

            for tc in range(T // 512):
                for dch in range(NDCH):
                    i = tc * NDCH + dch
                    par = i % 2
                    xv = xa[:, par * 512:(par + 1) * 512]
                    sq = xsq[:, par * 512:(par + 1) * 512]

                    lsem = "LA" if par == 0 else "LB"

                    def _ld(sync, i=i, tc=tc, dch=dch, xv=xv, lsem=lsem):
                        if i >= 2:
                            S.wait(sync, "Tm", tdone[i - 2])
                        S.inc(sync.dma_start(out=xv,
                                             in_=src_ap[dch * 128:(dch + 1) * 128,
                                                        tc * 512:(tc + 1) * 512]), lsem, 16)
                    self.Sy(_ld)
                    dmac = S.val(lsem)

                    def _sq(vector, xv=xv, sq=sq, dmac=dmac, lsem=lsem):
                        S.wait(vector, lsem, dmac)
                        S.inc(vector.tensor_tensor(sq, xv, xv, ALU.mult), "Vm")
                    self.V(_sq)
                    vdone[i] = S.val("Vm")

                    def _mm(tensor, tc=tc, dch=dch, xv=xv, sq=sq, i=i):
                        S.wait(tensor, "Vm", vdone[i])
                        if dch == 0 and tc >= 1:
                            S.wait(tensor, "Scm", scdone[tc - 1])
                        tensor.matmul(psA[0:1, :], onesf[:], xv,
                                      start=(dch == 0), stop=(dch == NDCH - 1))
                        S.inc(tensor.matmul(psB[0:1, :], onesf[:], sq,
                                            start=(dch == 0), stop=(dch == NDCH - 1)), "Tm")
                    self.T_(_mm)
                    tdone[i] = S.val("Tm")

                def _ev(scalar, tc=tc):
                    S.wait(scalar, "Tm", tdone[tc * NDCH + NDCH - 1])
                    scalar.copy(rows[0:1, tc * 512:(tc + 1) * 512], psA[0:1, :])
                    S.inc(scalar.copy(rows[0:1, T + tc * 512:T + (tc + 1) * 512], psB[0:1, :]),
                          "Scm")
                self.Sc(_ev)
                scdone[tc] = S.val("Scm")

            def _stats(vector):
                S.wait(vector, "Scm")
                mu = rows[0:1, 0:T]
                e2 = rows[0:1, T:2 * T]
                t3 = rows[0:1, 2 * T:3 * T]
                vector.tensor_scalar_mul(mu, mu, 1.0 / D)
                vector.tensor_scalar_mul(e2, e2, 1.0 / D)
                vector.tensor_tensor(t3, mu, mu, ALU.mult)
                vector.tensor_tensor(e2, e2, t3, ALU.subtract)
                vector.tensor_scalar_add(e2, e2, 1e-5)
                S.inc(vector.reciprocal(t3, e2), "Vm")
            self.V(_stats)

            def _sqrt(scalar):
                S.wait(scalar, "Vm")
                S.inc(scalar.activation(rows[0:1, T:2 * T], rows[0:1, 2 * T:3 * T],
                                        AF.Sqrt, bias=0.0, scale=1.0), "Scm")
            self.Sc(_sqrt)

            def _mur(vector):
                S.wait(vector, "Scm")
                S.inc(vector.tensor_tensor(rows[0:1, 2 * T:3 * T], rows[0:1, 0:T],
                                           rows[0:1, T:2 * T], ALU.mult), "Vm")
            self.V(_mur)

            prev_sc = 0
            for qb in range(T // 512):
                def _bc(tensor, qb=qb, prev_sc=prev_sc):
                    S.wait(tensor, "Vm")
                    S.wait(tensor, "Scm", prev_sc)
                    tensor.matmul(psC[:, :], self.c_sb["onesf_row"][:],
                                  rows[0:1, T + qb * 512:T + (qb + 1) * 512],
                                  start=True, stop=True)
                    S.inc(tensor.matmul(psD[:, :], self.c_sb["onesf_row"][:],
                                        rows[0:1, 2 * T + qb * 512:2 * T + (qb + 1) * 512],
                                        start=True, stop=True), "Tm")
                self.T_(_bc)
                tdn = S.val("Tm")

                def _ev2(scalar, qb=qb, tdn=tdn):
                    S.wait(scalar, "Tm", tdn)
                    scalar.copy(rbc[:, qb * 512:(qb + 1) * 512], psC[:, :])
                    S.inc(scalar.copy(murbc[:, qb * 512:(qb + 1) * 512], psD[:, :]), "Scm")
                self.Sc(_ev2)
                prev_sc = S.val("Scm")
            self.barrier()
        return rbc, murbc

    def _xn_fill(self, src, xn, xa, tmp, rbc, murbc, col0, width):
        S = self.S
        src_ap = self.dram[src]
        if not hasattr(self, "_xnf_hist"):
            self._xnf_hist = {}
        hist = self._xnf_hist.setdefault(id(xa), {0: 0, 1: 0})
        vdone = {-2: hist[0], -1: hist[1]}
        for tc in range(width // 512):
            for dch in range(NDCH):
                i = tc * NDCH + dch
                par = i % 2
                xv = xa[:, par * 512:(par + 1) * 512]

                lsem = "LA" if par == 0 else "LB"

                def _ld(sync, i=i, tc=tc, dch=dch, xv=xv, lsem=lsem):
                    if vdone[i - 2] > 0:
                        S.wait(sync, "Vm", vdone[i - 2])
                    S.inc(sync.dma_start(out=xv,
                                         in_=src_ap[dch * 128:(dch + 1) * 128,
                                                    col0 + tc * 512:col0 + (tc + 1) * 512]),
                          lsem, 16)
                self.Sy(_ld)
                dmac = S.val(lsem)

                def _v(vector, i=i, tc=tc, dch=dch, xv=xv, dmac=dmac, lsem=lsem):
                    S.wait(vector, lsem, dmac)
                    gsl = slice(col0 + tc * 512, col0 + (tc + 1) * 512)
                    lsl = slice(tc * 512, (tc + 1) * 512)
                    vector.tensor_tensor(tmp[:], xv, rbc[:, gsl], ALU.mult)
                    S.inc(vector.tensor_tensor(xn[dch][:, lsl], tmp[:], murbc[:, gsl],
                                               ALU.subtract), "Vm")
                self.V(_v)
                vdone[i] = S.val("Vm")
        n = width // 512 * NDCH
        hist[0] = vdone[n - 2] if n >= 2 else max(hist[0], vdone[n - 1])
        hist[1] = vdone[n - 1]

    def attention(self, flavor, src, dst):
        nc, S = self.nc, self.S
        with ExitStack() as outer:
            Eo = outer.enter_context
            nkv = H if flavor == "a" else G
            self.kfm = [Eo(nc.sbuf_tensor("kfm%s%d" % (flavor, i), [128, T], BF16))
                        for i in range(nkv)]
            self.vtm = [Eo(nc.sbuf_tensor("vtm%s%d" % (flavor, i), [128, nkv * 128], BF16))
                        for i in range(NQC)]
            if flavor == "a":
                self.gfm = Eo(nc.sbuf_tensor("at_gfm", [48, T], F32))
                self.kcmp = Eo(nc.sbuf_tensor("at_kcmp", [128, H * NB], BF16))
                self.vcmpT = Eo(nc.sbuf_tensor("at_vcmpT", [32, T], BF16))
            else:
                self.bias_sb = Eo(nc.sbuf_tensor("at_bias", [128, H + G], F32))
                self.obias = Eo(nc.sbuf_tensor("at_obias", [128, G], F32))
            with ExitStack() as inner:
                rbc, murbc = self._ln_stats(inner.enter_context, src, "at" + flavor)
                self._attn_kv(inner.enter_context, flavor, rbc, murbc, src)
                self.barrier()
            if flavor == "a":
                with ExitStack() as inner:
                    self._attn_cmp(inner.enter_context)
                    self.barrier()
            with ExitStack() as inner:
                self._attn_qc(inner.enter_context, flavor, src, dst)
                self.barrier()
        if flavor == "a":
            with ExitStack() as ctx2:
                self._attn_wo(ctx2.enter_context, src, dst)
                self.barrier()

    def _attn_kv(self, E, flavor, rbc, murbc, src):
        nc, S = self.nc, self.S
        nkv = H if flavor == "a" else G
        xn = [E(nc.sbuf_tensor("xnkv%s%d" % (flavor, i), [128, 512], BF16)) for i in range(NDCH)]
        xna = E(nc.sbuf_tensor("xnkva" + flavor, [128, 512 * 2], F32))
        xnt = E(nc.sbuf_tensor("xnkvt" + flavor, [128, 512], F32))
        wbuf = E(nc.sbuf_tensor("at_wbuf" + flavor, [128, 2048 * 2], BF16))
        evb = E(nc.sbuf_tensor("at_ev" + flavor, [64, 512 * 2], F32))
        qb_ = E(nc.sbuf_tensor("at_qb" + flavor, [128, 4 * 512], BF16))
        ps = [E(nc.psum_tensor("at_ps%s%d" % (flavor, i), [128, 512], F32)) for i in range(4)]
        if flavor == "a":
            cos = E(nc.sbuf_tensor("sc_ropecos", [64, T], BF16))
            sin = E(nc.sbuf_tensor("sc_ropesin", [64, T], BF16))
            wg = E(nc.sbuf_tensor("at_wg", [128, NDCH * 48], BF16))

            def _lg(gpsimd):
                S.inc(gpsimd.dma_start(out=cos[:], in_=self.ct["ropecos"].ap()), "GC", 16)
                S.inc(gpsimd.dma_start(out=sin[:], in_=self.ct["ropesin"].ap()), "GC", 16)
                S.inc(gpsimd.dma_start(out=wg[:].rearrange("p (i n) -> p i n", n=48),
                                       in_=self.dram["aWg"].rearrange("(i p) n -> p i n", p=128)),
                      "GC", 16)
            self.Gp(_lg)
        gw_dma = S.val("GC")

        if flavor == "b":
            def _ldb(sync):
                S.inc(sync.dma_start(out=self.bias_sb[:, 0:H], in_=self.dram["bbq"]), "SB", 16)
                S.inc(sync.dma_start(out=self.bias_sb[:, H:H + G], in_=self.dram["bbk"]), "SB", 16)
                S.inc(sync.dma_start(out=self.obias[:], in_=self.dram["bbv"]), "SB", 16)
            self.Sy(_ldb)

        bias_dma = S.val("SB")
        wq_nm = "aWq" if flavor == "a" else "bWq"
        wk_nm = "aWk" if flavor == "a" else "bWk"
        wv_nm = "aWv" if flavor == "a" else "bWv"
        qk_jobs = [(wq_nm, h, "q") for h in range(H)] + [(wk_nm, h, "k") for h in range(nkv)]
        wcols = 128
        nhb = (nkv * 128) // wcols

        wt_free = {0: 0, 1: 0}
        bank_free = {0: 0, 1: 0, 2: 0, 3: 0}

        for tc in range(4):
            self._xn_fill(src, xn, xna, xnt, rbc, murbc, tc * 512, 512)
            xnr = S.val("Vm")
            tsl = slice(tc * 512, (tc + 1) * 512)
            ji_tot = 0
            for ji, (wnm, hdch, kind) in enumerate(qk_jobs):
                par = ji % 2
                wt = wbuf[:, par * 2048:(par + 1) * 2048]

                gsem = "GA" if par == 0 else "GB"

                def _lw(gpsimd, wnm=wnm, hdch=hdch, wt=wt, par=par, gsem=gsem):
                    S.wait(gpsimd, "Tm", wt_free[par])
                    S.inc(gpsimd.dma_start(
                        out=wt.rearrange("p (i n) -> p i n", n=128),
                        in_=self.dram[wnm][:, hdch * 128:(hdch + 1) * 128]
                            .rearrange("(i p) n -> p i n", p=128)), gsem, 16)
                self.Gp(_lw)
                gmc = S.val(gsem)

                def _mm(tensor, wt=wt, gmc=gmc, ji=ji, gsem=gsem, xnr=xnr):
                    S.wait(tensor, gsem, gmc)
                    S.wait(tensor, "Vm", max(bank_free[ji % 2], xnr))
                    bank = ps[ji % 2]
                    for dch in range(NDCH):
                        m = tensor.matmul(bank[:], wt[:, dch * 128:(dch + 1) * 128],
                                          xn[dch][:], start=(dch == 0), stop=(dch == NDCH - 1))
                    S.inc(m, "Tm")
                self.T_(_mm)
                wt_free[par] = S.val("Tm")
                tdn = S.val("Tm")

                def _ev(vector, hdch=hdch, kind=kind, tdn=tdn, flavor=flavor, ji=ji,
                        tsl=tsl):
                    S.wait(vector, "Tm", tdn)
                    if flavor == "a":
                        S.wait(vector, "GC", gw_dma)
                    else:
                        S.wait(vector, "SB", bias_dma)
                    if kind == "q" and hdch % 4 == 0:
                        S.wait(vector, "SA")
                    bank = ps[ji % 2]
                    if kind == "q":
                        dst, dsl = qb_, slice((hdch % 4) * 512, (hdch % 4 + 1) * 512)
                    else:
                        dst, dsl = self.kfm[hdch], tsl
                    if flavor == "a":
                        c1, s1_ = cos[:, tsl], sin[:, tsl]
                        t1, t2 = evb[0:64, 0:512], evb[0:64, 512:1024]
                        vector.tensor_tensor(t1, bank[0:64, :], c1, ALU.mult)
                        vector.tensor_tensor(t2, bank[64:128, :], s1_, ALU.mult)
                        vector.tensor_tensor(dst[0:64, dsl], t1, t2, ALU.subtract)
                        vector.tensor_tensor(t1, bank[0:64, :], s1_, ALU.mult)
                        vector.tensor_tensor(t2, bank[64:128, :], c1, ALU.mult)
                        m = vector.tensor_tensor(dst[64:128, dsl], t1, t2, ALU.add)
                    else:
                        bcol = (self.bias_sb[:, hdch:hdch + 1] if kind == "q"
                                else self.bias_sb[:, H + hdch:H + hdch + 1])
                        m = vector.tensor_scalar(dst[:, dsl], bank[:], bcol, None, ALU.add)
                    S.inc(m, "Vm")
                self.V(_ev)
                bank_free[ji % 2] = S.val("Vm")
                evq = S.val("Vm")

                if kind == "q":
                    def _st(sync, hdch=hdch, tc=tc, evq=evq):
                        S.wait(sync, "Vm", evq)
                        S.inc(sync.dma_start(
                            out=self.dram["qS"][hdch * 128:(hdch + 1) * 128, tc * 512:(tc + 1) * 512],
                            in_=qb_[:, (hdch % 4) * 512:(hdch % 4 + 1) * 512]), "SA", 16)
                    self.Sy(_st)

            for hb in range(nhb):
                par = hb % 2
                wt = wbuf[:, par * 2048:par * 2048 + NDCH * wcols]

                gsem = "GA" if par == 0 else "GB"

                def _lwv(gpsimd, hb=hb, wt=wt, par=par, gsem=gsem):
                    S.wait(gpsimd, "Tm", wt_free[par])
                    S.inc(gpsimd.dma_start(
                        out=wt.rearrange("p (i n) -> p i n", n=wcols),
                        in_=self.dram[wv_nm][:, hb * wcols:(hb + 1) * wcols]
                            .rearrange("(i p) n -> p i n", p=128)), gsem, 16)
                self.Gp(_lwv)
                gmc = S.val(gsem)

                for lt in range(4):
                    tt = tc * 4 + lt
                    def _mmv(tensor, tt=tt, lt=lt, wt=wt, gmc=gmc, gsem=gsem, xnr=xnr):
                        S.wait(tensor, gsem, gmc)
                        S.wait(tensor, "Vm", max(bank_free[2 + lt % 2], xnr))
                        bank = ps[2 + lt % 2]
                        for dch in range(NDCH):
                            m = tensor.matmul(bank[:, 0:wcols],
                                              xn[dch][:, lt * 128:(lt + 1) * 128],
                                              wt[:, dch * wcols:(dch + 1) * wcols],
                                              start=(dch == 0), stop=(dch == NDCH - 1))
                        S.inc(m, "Tm")
                    self.T_(_mmv)
                    if lt >= 2:
                        wt_free[par] = S.val("Tm")
                    tdn = S.val("Tm")

                    def _evv(vector, hb=hb, tt=tt, lt=lt, tdn=tdn):
                        S.wait(vector, "Tm", tdn)
                        S.inc(vector.tensor_copy(self.vtm[tt][:, hb * wcols:(hb + 1) * wcols],
                                                 ps[2 + lt % 2][:, 0:wcols]), "Vm")
                    self.V(_evv)
                    bank_free[2 + lt % 2] = S.val("Vm")

            if flavor == "a":
                def _mmg(tensor, tc=tc, gw_dma=gw_dma, xnr=xnr):
                    S.wait(tensor, "GC", gw_dma)
                    S.wait(tensor, "Vm", xnr)
                    S.wait(tensor, "Scm")
                    bank = ps[0]
                    for dch in range(NDCH):
                        m = tensor.matmul(bank[0:48, :], wg[:, dch * 48:(dch + 1) * 48],
                                          xn[dch][:], start=(dch == 0), stop=(dch == NDCH - 1))
                    S.inc(m, "Tm")
                self.T_(_mmg)
                tdn = S.val("Tm")

                def _evg(scalar, tc=tc, tdn=tdn):
                    S.wait(scalar, "Tm", tdn)
                    S.inc(scalar.activation(self.gfm[0:48, tc * 512:(tc + 1) * 512],
                                            ps[0][0:48, :], AF.Sigmoid, bias=0.0, scale=1.0),
                          "Scm")
                self.Sc(_evg)
                bank_free[0] = S.val("Scm")

            if flavor == "a" and tc == 3:
                def _stk(sync):
                    S.wait(sync, "Vm")
                    S.wait(sync, "Scm")
                    for hh in range(H):
                        S.inc(sync.dma_start(out=self.dram["kS"][hh * 128:(hh + 1) * 128, :],
                                             in_=self.kfm[hh][:]), "SB", 16)
                    S.inc(sync.dma_start(out=self.dram["gS"], in_=self.gfm[:]), "SB", 16)
                self.Sy(_stk)
            bank_free[0] = S.val("Vm")

    def _attn_cmp(self, E):
        nc, S = self.nc, self.S
        ktm = [E(nc.sbuf_tensor("at_ktm%d" % i, [128, T], BF16)) for i in range(NQC)]
        ps = [E(nc.psum_tensor("cmp_ps%d" % i, [128, 512], F32)) for i in range(2)]
        Ablk = E(nc.sbuf_tensor("sb_Ablk", [128, NQC * NB], BF16))

        def _lda(gpsimd):
            S.inc(gpsimd.dma_start(out=Ablk[:], in_=self.ct["Ablk"].ap()), "GA", 16)
        self.Gp(_lda)
        abdma = S.val("GA")

        def _wa(tensor):
            S.wait(tensor, "GA", abdma)
        self.T_(_wa)
        for cb in range(4):
            prev_sc = S.val("Scm")

            def _mm(tensor, cb=cb, prev_sc=prev_sc):
                S.wait(tensor, "Scm", prev_sc)
                for tt in range(NQC):
                    m = tensor.matmul(ps[0][0:32, :], Ablk[:, tt * NB:(tt + 1) * NB],
                                      self.vtm[tt][:, cb * 512:(cb + 1) * 512],
                                      start=(tt == 0), stop=(tt == NQC - 1))
                S.inc(m, "Tm")
            self.T_(_mm)
            tdn = S.val("Tm")

            def _evc(scalar, cb=cb, tdn=tdn):
                S.wait(scalar, "Tm", tdn)
                S.inc(scalar.copy(self.vcmpT[0:32, cb * 512:(cb + 1) * 512],
                                  ps[0][0:32, :]), "Scm")
            self.Sc(_evc)

        def _lt(sync):
            for tt in range(NQC):
                S.inc(sync.dma_start(out=ktm[tt][:],
                                     in_=self.dram["kS"][:, tt * 128:(tt + 1) * 128],
                                     transpose=True), "LA", 16)
        self.Sy(_lt)
        kdma = S.val("LA")

        def _mmk(tensor, kdma=kdma):
            S.wait(tensor, "LA", kdma)
            for h in range(H):
                for tt in range(NQC):
                    m = tensor.matmul(ps[1][:, h * NB:(h + 1) * NB],
                                      ktm[tt][:, h * 128:(h + 1) * 128],
                                      Ablk[:, tt * NB:(tt + 1) * NB],
                                      start=(h == 0 and tt == 0),
                                      stop=(h == H - 1 and tt == NQC - 1))
            S.inc(m, "Tm")
        self.T_(_mmk)

        def _ek(scalar):
            S.wait(scalar, "Tm")
            S.inc(scalar.copy(self.kcmp[:], ps[1][:]), "Scm")
        self.Sc(_ek)
        self._dbg("kcmp", self.kcmp[:], 128, H * NB)
        self._dbg("vcmpT", self.vcmpT[:], 32, T)

    def _attn_qc(self, E, flavor, src, dst):
        nc, S = self.nc, self.S
        cs = self.c_sb
        qqc = E(nc.sbuf_tensor("qc_q" + flavor, [128, T], BF16))
        grow = E(nc.sbuf_tensor("qc_grow" + flavor, [1, 48 * 128], BF16))
        pT = E(nc.sbuf_tensor("qc_pT" + flavor, [128, T], BF16))
        pTw = E(nc.sbuf_tensor("qc_pTw" + flavor, [128, 256], BF16))
        sw_t = E(nc.sbuf_tensor("qc_swt" + flavor, [128, 256], F32))
        ostrip = E(nc.sbuf_tensor("qc_o" + flavor, [128, T], BF16))
        rowsb = E(nc.sbuf_tensor("qc_rows" + flavor, [1, 512], F32))
        rowbf = E(nc.sbuf_tensor("qc_rowbf" + flavor, [1, 384], BF16))
        bc_sb = E(nc.sbuf_tensor("qc_bc" + flavor, [128, 384], F32))
        otmp = E(nc.sbuf_tensor("qc_otmp" + flavor, [128, 384], F32))
        if flavor == "a":
            for nm, shp, dt_ in (("cmpmask", [128, NQC * NB], F32),
                                 ("anyoff", [128, NQC], F32),
                                 ("curblk", [128, NQC * NB], F32),
                                 ("Eexp", [NB, T], BF16)):
                cs[nm] = E(nc.sbuf_tensor("qcc_" + flavor + nm, shp, dt_))
        for nm in ("diagmask", "swadiag", "swaprev"):
            cs[nm] = E(nc.sbuf_tensor("qcc_" + flavor + nm, [128, 128], F32))

        def _ldm(gpsimd, flavor=flavor):
            nms = ("cmpmask", "anyoff", "curblk", "Eexp") if flavor == "a" else ()
            for nm in nms + ("diagmask", "swadiag", "swaprev"):
                S.inc(gpsimd.dma_start(out=cs[nm][:], in_=self.ct[nm].ap()), "GC", 16)
        self.Gp(_ldm)
        mdma = S.val("GC")

        def _wm(tensor):
            S.wait(tensor, "GC", mdma)
        self.T_(_wm)

        def _wm2(vector):
            S.wait(vector, "GC", mdma)
        self.V(_wm2)
        if flavor == "a":
            ecmp = E(nc.sbuf_tensor("qc_ecmp", [128, 512], F32))
            pcmp = E(nc.sbuf_tensor("qc_pcmp", [128, 512], F32))
            pcmp_bf = E(nc.sbuf_tensor("qc_pcmpb", [128, 512], BF16))
            imp = E(nc.sbuf_tensor("qc_imp", [128, 512], F32))
            selacc = E(nc.sbuf_tensor("qc_sel", [128, 512], F32))
            mx16 = E(nc.sbuf_tensor("qc_mx", [128, 16], F32))
            mxsel = E(nc.sbuf_tensor("qc_mxsel", [128, 16], F32))
            cmprec = E(nc.sbuf_tensor("qc_cmprec", [128, 16], F32))
            selb = E(nc.sbuf_tensor("qc_selb", [128, 512], BF16))
            selbT = E(nc.sbuf_tensor("qc_selbT", [32, T], BF16))
            pcmpT = E(nc.sbuf_tensor("qc_pcmpT", [32, T], BF16))
        if flavor == "b":
            xcol = E(nc.sbuf_tensor("qc_xcol", [128, T], F32))
            xnew = E(nc.sbuf_tensor("qc_xnew", [128, T], F32))
        psS = [E(nc.psum_tensor("qc_psS" + flavor + str(i), [128, 512], F32)) for i in range(4)]
        psO = E(nc.psum_tensor("qc_psO" + flavor, [128, 384], F32))
        psR = E(nc.psum_tensor("qc_psR" + flavor, [1, 512], F32))
        psB = E(nc.psum_tensor("qc_psB" + flavor, [128, 384], F32))
        psX = E(nc.psum_tensor("qc_psX" + flavor, [128, 512], F32))

        def trb(h):
            bank = (psX, psS[0], psS[1], psS[2])[h // 4]
            return bank[0:32, (h % 4) * 128:(h % 4 + 1) * 128]

        for qc in range(NQC):
            def _lq(sync, qc=qc, flavor=flavor):
                S.wait(sync, "Vm")
                S.inc(sync.dma_start(out=qqc[:].rearrange("p (i n) -> p i n", n=128),
                                     in_=self.dram["qS"][:, qc * 128:(qc + 1) * 128]
                                     .rearrange("(i p) n -> p i n", p=128)), "LA", 16)
                if flavor == "b":
                    S.inc(sync.dma_start(out=xcol[:].rearrange("p (i n) -> p i n", n=128),
                                         in_=self.dram[src][:, qc * 128:(qc + 1) * 128]
                                         .rearrange("(i p) n -> p i n", p=128)), "LA", 16)
            self.Sy(_lq)
            qdma = S.val("LA")

            if flavor == "a":
                def _lg(gpsimd, qc=qc):
                    S.wait(gpsimd, "Vm")
                    S.inc(gpsimd.dma_start(
                        out=grow[0:1, :].rearrange("p (g n) -> p g n", n=128),
                        in_=self.dram["gS"][:, qc * 128:(qc + 1) * 128].unsqueeze(0)),
                        "GA", 16)
                self.Gp(_lg)
            gdma = S.val("GA")

            if flavor == "a":
                def _scmp(tensor, qc=qc, qdma=qdma):
                    S.wait(tensor, "LA", qdma)
                    S.wait(tensor, "Vm")
                    for h in range(H):
                        m = tensor.matmul(psX[:, h * NB:(h + 1) * NB],
                                          qqc[:, h * 128:(h + 1) * 128],
                                          self.kcmp[:, h * NB:(h + 1) * NB],
                                          start=True, stop=True)
                    S.inc(m, "Tm")
                self.T_(_scmp)
                tdn = S.val("Tm")

                def _cm1(vector, qc=qc, tdn=tdn):
                    S.wait(vector, "Tm", tdn)
                    S.inc(vector.tensor_tensor(
                        ecmp[:].rearrange("p (h n) -> p h n", n=NB),
                        psX[:].rearrange("p (h n) -> p h n", n=NB),
                        cs["cmpmask"][:, qc * NB:(qc + 1) * NB][:, None, :]
                        .broadcast_to([128, H, NB]), ALU.add), "Vm")
                self.V(_cm1)
                if self._dbgq == qc:
                    self._dbg("sraw", ecmp[:], 128, 512)

                def _cm2(scalar):
                    S.wait(scalar, "Vm")
                    S.inc(scalar.activation(ecmp[:], ecmp[:], AF.Exp, bias=0.0, scale=SCALE),
                          "Scm")
                self.Sc(_cm2)

                def _cm3(vector, qc=qc):
                    S.wait(vector, "Scm")
                    vector.reduce_sum(cmprec[:][:, :, None],
                                      ecmp[:].rearrange("p (h n) -> p h n", n=NB), axis=AXX)
                    vector.tensor_scalar(cmprec[:], cmprec[:], cs["anyoff"][:, qc:qc + 1],
                                         None, ALU.add)
                    vector.tensor_copy(mxsel[:], cmprec[:])
                    vector.reciprocal(mx16[:], cmprec[:])
                    vector.tensor_tensor(imp[:].rearrange("p (h n) -> p h n", n=NB),
                                         cs["curblk"][:, qc * NB:(qc + 1) * NB][:, None, :]
                                         .broadcast_to([128, H, NB]),
                                         mxsel[:][:, :, None].broadcast_to([128, H, NB]),
                                         ALU.mult)
                    vector.tensor_tensor(imp[:], imp[:], ecmp[:], ALU.add)
                    vector.tensor_tensor(pcmp[:].rearrange("p (h n) -> p h n", n=NB),
                                         ecmp[:].rearrange("p (h n) -> p h n", n=NB),
                                         mx16[:][:, :, None].broadcast_to([128, H, NB]),
                                         ALU.mult)
                    S.inc(vector.tensor_copy(pcmp_bf[:], pcmp[:]), "Vm")
                self.V(_cm3)
                if self._dbgq == qc:
                    self._dbg("pcmp", pcmp[:], 128, 512)
                    self._dbg("imp", imp[:], 128, 512)

                if qc >= 4:
                    def _sel(vector, qc=qc):
                        vector.memset(selacc[:], 0.0)
                        for it in range(8):
                            vector.reduce_max(mx16[:][:, :, None],
                                              imp[:].rearrange("p (h n) -> p h n", n=NB),
                                              axis=AXX)
                            vector.tensor_tensor(ecmp[:].rearrange("p (h n) -> p h n", n=NB),
                                                 imp[:].rearrange("p (h n) -> p h n", n=NB),
                                                 mx16[:][:, :, None].broadcast_to([128, H, NB]),
                                                 ALU.is_ge)
                            vector.tensor_tensor(selacc[:], selacc[:], ecmp[:], ALU.max)
                            vector.tensor_tensor(ecmp[:].rearrange("p (h n) -> p h n", n=NB),
                                                 ecmp[:].rearrange("p (h n) -> p h n", n=NB),
                                                 mxsel[:][:, :, None].broadcast_to([128, H, NB]),
                                                 ALU.mult)
                            vector.scalar_tensor_tensor(imp[:], ecmp[:], -4.0, imp[:],
                                                        ALU.mult, ALU.add)
                        S.inc(vector.tensor_scalar(selb[:], selacc[:], -1.0, -NEG,
                                                   ALU.add, ALU.mult), "Vm")
                    self.V(_sel)
                    if self._dbgq == qc:
                        self._dbg("selb", selb[:], 128, 512)
                seldone = S.val("Vm")

                def _tr(tensor, seldone=seldone):
                    S.wait(tensor, "Vm", seldone)
                    for h in range(H):
                        m = tensor.matmul(trb(h), pcmp_bf[:, h * NB:(h + 1) * NB],
                                          cs["ident"][:], start=True, stop=True)
                    S.inc(m, "Tm")
                self.T_(_tr)
                tdn = S.val("Tm")

                def _etr(vector, tdn=tdn):
                    S.wait(vector, "Tm", tdn)
                    vector.tensor_copy(pcmpT[0:32, 0:512], psX[0:32, 0:512])
                    vector.tensor_copy(pcmpT[0:32, 512:1024], psS[0][0:32, :])
                    vector.tensor_copy(pcmpT[0:32, 1024:1536], psS[1][0:32, :])
                    S.inc(vector.tensor_copy(pcmpT[0:32, 1536:2048], psS[2][0:32, :]), "Vm")
                self.V(_etr)
                if self._dbgq == qc:
                    self._dbg("pcmpT", pcmpT[:], 32, T)
                petr = S.val("Vm")

                if qc >= 4:
                    def _tr2(tensor, petr=petr):
                        S.wait(tensor, "Vm", petr)
                        for h in range(H):
                            m = tensor.matmul(trb(h), selb[:, h * NB:(h + 1) * NB],
                                              cs["ident"][:], start=True, stop=True)
                        S.inc(m, "Tm")
                    self.T_(_tr2)
                    tdn2 = S.val("Tm")

                    def _etr2(vector, tdn2=tdn2):
                        S.wait(vector, "Tm", tdn2)
                        vector.tensor_copy(selbT[0:32, 0:512], psX[0:32, 0:512])
                        vector.tensor_copy(selbT[0:32, 512:1024], psS[0][0:32, :])
                        vector.tensor_copy(selbT[0:32, 1024:1536], psS[1][0:32, :])
                        S.inc(vector.tensor_copy(selbT[0:32, 1536:2048], psS[2][0:32, :]), "Vm")
                    self.V(_etr2)
                    if self._dbgq == qc:
                        self._dbg("selbT", selbT[:], 32, T)

            for h in range(H):
                kv = h if flavor == "a" else h // (H // G)

                def _qk(tensor, qc=qc, h=h, kv=kv, flavor=flavor, qdma=qdma):
                    S.wait(tensor, "LA", qdma)
                    S.wait(tensor, "Vm")
                    has_bias = (flavor == "a" and qc >= 4)
                    for kt in range(qc + 1):
                        first_in_bank = (kt % 4 == 0)
                        last_in_bank = (kt == qc or kt % 4 == 3)
                        m = tensor.matmul(psS[kt // 4][:, (kt % 4) * 128:(kt % 4 + 1) * 128],
                                          self.kfm[kv][:, kt * 128:(kt + 1) * 128],
                                          qqc[:, h * 128:(h + 1) * 128],
                                          start=first_in_bank,
                                          stop=(last_in_bank and not has_bias))
                    if flavor == "a":
                        if qc >= 1:
                            tensor.matmul(psX[:, 0:128],
                                          self.kfm[kv][:, (qc - 1) * 128:qc * 128],
                                          qqc[:, h * 128:(h + 1) * 128],
                                          start=True, stop=(qc < 1))
                        m = tensor.matmul(psX[:, 128:256],
                                          self.kfm[kv][:, qc * 128:(qc + 1) * 128],
                                          qqc[:, h * 128:(h + 1) * 128],
                                          start=(qc == 0), stop=True)
                    S.inc(m, "Tm")
                self.T_(_qk)
                qkdone = S.val("Tm")

                swadone = None
                if flavor == "a":
                    prev_swadone = getattr(self, "_prev_swadone", 0)

                    def _swa(vector, qc=qc, qkdone=qkdone, prev_swadone=prev_swadone):
                        S.wait(vector, "Tm", qkdone)
                        if prev_swadone > 0:
                            S.wait(vector, "Scm", prev_swadone)
                        vector.tensor_tensor(sw_t[:, 128:256], psX[:, 128:256],
                                             cs["swadiag"][:], ALU.add)
                        if qc >= 1:
                            m = vector.tensor_tensor(sw_t[:, 0:128], psX[:, 0:128],
                                                     cs["swaprev"][:], ALU.add)
                        else:
                            m = vector.memset(sw_t[:, 0:128], NEG)
                        S.inc(m, "Vm")
                    self.V(_swa)
                    swv = S.val("Vm")

                    def _swae(scalar, swv=swv):
                        S.wait(scalar, "Vm", swv)
                        S.inc(scalar.activation(pTw[:], sw_t[:], AF.Exp, bias=0.0, scale=SCALE),
                              "Scm")
                    self.Sc(_swae)
                    swadone = S.val("Scm")
                    self._prev_swadone = swadone

                    if qc >= 4:
                        def _bias(tensor, qc=qc, h=h, swadone=swadone):
                            S.wait(tensor, "Scm", swadone)
                            for kt in range(qc + 1):
                                last_in_bank = (kt == qc or kt % 4 == 3)
                                m = tensor.matmul(
                                    psS[kt // 4][:, (kt % 4) * 128:(kt % 4 + 1) * 128],
                                    cs["Eexp"][:, kt * 128:(kt + 1) * 128],
                                    selbT[:, h * 128:(h + 1) * 128],
                                    start=False, stop=last_in_bank)
                            S.inc(m, "Tm")
                        self.T_(_bias)
                biasdone = S.val("Tm")

                def _diag(vector, qc=qc, biasdone=biasdone, swadone=swadone, flavor=flavor):
                    S.wait(vector, "Tm", biasdone)
                    if swadone is not None:
                        S.wait(vector, "Scm", swadone)
                    kt = qc
                    S.inc(vector.tensor_tensor(
                        psS[kt // 4][:, (kt % 4) * 128:(kt % 4 + 1) * 128],
                        psS[kt // 4][:, (kt % 4) * 128:(kt % 4 + 1) * 128],
                        cs["diagmask"][:], ALU.add), "Vm")
                self.V(_diag)
                dmdone = S.val("Vm")

                def _exps(scalar, qc=qc, dmdone=dmdone):
                    S.wait(scalar, "Vm", dmdone)
                    for kt in range(qc + 1):
                        m = scalar.activation(pT[:, kt * 128:(kt + 1) * 128],
                                              psS[kt // 4][:, (kt % 4) * 128:(kt % 4 + 1) * 128],
                                              AF.Exp, bias=0.0, scale=SCALE)
                    S.inc(m, "Scm")
                self.Sc(_exps)
                expdone = S.val("Scm")

                def _sums(tensor, qc=qc, expdone=expdone, flavor=flavor):
                    S.wait(tensor, "Scm", expdone)
                    for kt in range(qc + 1):
                        m = tensor.matmul(psR[0:1, 0:128], cs["ones_col"][:],
                                          pT[:, kt * 128:(kt + 1) * 128],
                                          start=(kt == 0), stop=(kt == qc))
                    if flavor == "a":
                        tensor.matmul(psR[0:1, 128:256], cs["ones_col"][:], pTw[:, 0:128],
                                      start=True, stop=False)
                        m = tensor.matmul(psR[0:1, 128:256], cs["ones_col"][:], pTw[:, 128:256],
                                          start=False, stop=True)
                    S.inc(m, "Tm")
                self.T_(_sums)
                sumdone = S.val("Tm")

                def _rec(vector, qc=qc, h=h, sumdone=sumdone, flavor=flavor, gdma=gdma):
                    S.wait(vector, "Tm", sumdone)
                    if flavor == "a":
                        S.wait(vector, "GA", gdma)
                        vector.reciprocal(rowsb[0:1, 0:256], psR[0:1, 0:256])
                        vector.tensor_scalar_add(otmp[:, 0:64], otmp[:, 0:64], 0.0)
                        vector.tensor_tensor(rowbf[0:1, 0:128], rowsb[0:1, 0:128],
                                             grow[0:1, (16 + h) * 128:(17 + h) * 128],
                                             ALU.mult)
                        vector.tensor_tensor(rowbf[0:1, 128:256], rowsb[0:1, 128:256],
                                             grow[0:1, (32 + h) * 128:(33 + h) * 128],
                                             ALU.mult)
                        m = vector.tensor_copy(rowbf[0:1, 256:384],
                                               grow[0:1, h * 128:(h + 1) * 128])
                    else:
                        vector.reciprocal(rowsb[0:1, 0:128], psR[0:1, 0:128])
                        vector.tensor_scalar_add(otmp[:, 0:64], otmp[:, 0:64], 0.0)
                        m = vector.tensor_copy(rowbf[0:1, 0:128], rowsb[0:1, 0:128])
                    S.inc(m, "Vm")
                self.V(_rec)
                recdone = S.val("Vm")

                def _bc(tensor, recdone=recdone, flavor=flavor):
                    S.wait(tensor, "Vm", recdone)
                    S.wait(tensor, "Scm")
                    n = 384 if flavor == "a" else 128
                    S.inc(tensor.matmul(psB[:, 0:n], cs["ones_row"][:], rowbf[0:1, 0:n],
                                        start=True, stop=True), "Tm")
                self.T_(_bc)
                bcdone = S.val("Tm")

                def _ebc(scalar, bcdone=bcdone, flavor=flavor):
                    S.wait(scalar, "Tm", bcdone)
                    n = 384 if flavor == "a" else 128
                    S.inc(scalar.copy(bc_sb[:, 0:n], psB[:, 0:n]), "Scm")
                self.Sc(_ebc)
                ebcdone = S.val("Scm")

                def _pv(tensor, qc=qc, h=h, kv=kv, flavor=flavor):
                    for kt in range(qc + 1):
                        m = tensor.matmul(psO[:, 0:128],
                                          self.vtm[kt][:, kv * 128:(kv + 1) * 128],
                                          pT[:, kt * 128:(kt + 1) * 128],
                                          start=(kt == 0), stop=(kt == qc))
                    if flavor == "a":
                        if qc >= 1:
                            tensor.matmul(psO[:, 128:256],
                                          self.vtm[qc - 1][:, kv * 128:(kv + 1) * 128],
                                          pTw[:, 0:128], start=True, stop=False)
                        m = tensor.matmul(psO[:, 128:256],
                                          self.vtm[qc][:, kv * 128:(kv + 1) * 128],
                                          pTw[:, 128:256], start=(qc == 0), stop=True)
                        m = tensor.matmul(psO[:, 256:384], self.vcmpT[:, h * 128:(h + 1) * 128],
                                          pcmpT[:, h * 128:(h + 1) * 128], start=True, stop=True)
                    S.inc(m, "Tm")
                self.T_(_pv)
                pvdone = S.val("Tm")

                def _oev(vector, qc=qc, h=h, kv=kv, pvdone=pvdone, ebcdone=ebcdone,
                         flavor=flavor):
                    S.wait(vector, "Tm", pvdone)
                    S.wait(vector, "Scm", ebcdone)
                    if flavor == "a":
                        vector.tensor_tensor(otmp[:, 0:384], psO[:, 0:384], bc_sb[:, 0:384],
                                             ALU.mult)
                        vector.tensor_tensor(otmp[:, 0:128], otmp[:, 0:128], otmp[:, 128:256],
                                             ALU.add)
                        vector.tensor_tensor(otmp[:, 0:128], otmp[:, 0:128], otmp[:, 256:384],
                                             ALU.add)
                        m = vector.tensor_copy(ostrip[:, h * 128:(h + 1) * 128], otmp[:, 0:128])
                    else:
                        vector.tensor_tensor(otmp[:, 0:128], psO[:, 0:128], bc_sb[:, 0:128],
                                             ALU.mult)
                        vector.tensor_scalar(otmp[:, 0:128], otmp[:, 0:128],
                                             self.obias[:, kv:kv + 1], None, ALU.add)
                        m = vector.tensor_tensor(xnew[:, h * 128:(h + 1) * 128],
                                                 xcol[:, h * 128:(h + 1) * 128],
                                                 otmp[:, 0:128], ALU.add)
                    S.inc(m, "Vm")
                self.V(_oev)

            if self._dbgq == qc and flavor == "a":
                self._dbg("ostrip", ostrip[:], 128, T)
                self._dbg("qqc", qqc[:], 128, T)
                self._dbg("grow", grow[0:1, :], 1, 48 * 128)

            def _sto(sync, qc=qc, flavor=flavor):
                S.wait(sync, "Vm")
                if flavor == "a":
                    S.inc(sync.dma_start(out=self.dram["oS"][:, qc * 128:(qc + 1) * 128]
                                         .rearrange("(i p) n -> p i n", p=128),
                                         in_=ostrip[:].rearrange("p (i n) -> p i n", n=128)),
                          "SA", 16)
                else:
                    S.inc(sync.dma_start(out=self.dram[dst][:, qc * 128:(qc + 1) * 128]
                                         .rearrange("(i p) n -> p i n", p=128),
                                         in_=xnew[:].rearrange("p (i n) -> p i n", n=128)),
                          "SA", 16)
            self.Sy(_sto)
            std = S.val("SA")

            def _wst(vector, std=std):
                S.wait(vector, "SA", std)
            self.V(_wst)

    def _attn_wo(self, E, src, dst):
        nc, S = self.nc, self.S
        ocols = E(nc.sbuf_tensor("wo_o", [128, NDCH * 512], BF16))
        xcols = E(nc.sbuf_tensor("wo_x", [128, NDCH * 512], F32))
        xout = E(nc.sbuf_tensor("wo_xo", [128, NDCH * 512], F32))
        wo = E(nc.sbuf_tensor("wo_w", [128, 2048 * 2], BF16))
        psW = [E(nc.psum_tensor("wo_ps%d" % i, [128, 512], F32)) for i in range(2)]

        for qb in range(4):
            prev_dmo = S.val("SA")

            def _lx(sync, qb=qb, prev_dmo=prev_dmo):
                if qb >= 1:
                    S.wait(sync, "SA", prev_dmo)
                S.inc(sync.dma_start(out=ocols[:].rearrange("p (i n) -> p i n", n=512),
                                     in_=self.dram["oS"][:, qb * 512:(qb + 1) * 512]
                                     .rearrange("(i p) n -> p i n", p=128)), "LA", 16)
                S.inc(sync.dma_start(out=xcols[:].rearrange("p (i n) -> p i n", n=512),
                                     in_=self.dram[src][:, qb * 512:(qb + 1) * 512]
                                     .rearrange("(i p) n -> p i n", p=128)), "LA", 16)
            self.Sy(_lx)
            xdma = S.val("LA")

            evd = {-2: S.val("Vm"), -1: S.val("Vm")}
            for dch in range(NDCH):
                par = dch % 2
                prev_tm = S.val("Tm")

                gsem = "GA" if par == 0 else "GB"

                def _lw(gpsimd, qb=qb, dch=dch, par=par, prev_tm=prev_tm, gsem=gsem):
                    S.wait(gpsimd, "Tm", prev_tm)
                    S.inc(gpsimd.dma_start(
                        out=wo[:, par * 2048:(par + 1) * 2048].rearrange("p (i n) -> p i n", n=128),
                        in_=self.dram["aWo"][:, dch * 128:(dch + 1) * 128]
                            .rearrange("(i p) n -> p i n", p=128)), gsem, 16)
                self.Gp(_lw)
                gmc = S.val(gsem)

                def _mm(tensor, qb=qb, dch=dch, par=par, gmc=gmc, xdma=xdma, gsem=gsem):
                    S.wait(tensor, gsem, gmc)
                    S.wait(tensor, "LA", xdma)
                    S.wait(tensor, "Vm", evd[dch - 2])
                    for hd in range(NDCH):
                        m = tensor.matmul(psW[par][:],
                                          wo[:, par * 2048 + hd * 128:par * 2048 + (hd + 1) * 128],
                                          ocols[:, hd * 512:(hd + 1) * 512],
                                          start=(hd == 0), stop=(hd == NDCH - 1))
                    S.inc(m, "Tm")
                self.T_(_mm)
                tdn = S.val("Tm")

                def _ev(vector, dch=dch, par=par, tdn=tdn, prev_dmo=prev_dmo):
                    S.wait(vector, "Tm", tdn)
                    S.wait(vector, "SA", prev_dmo)
                    S.inc(vector.tensor_tensor(xout[:, dch * 512:(dch + 1) * 512],
                                               xcols[:, dch * 512:(dch + 1) * 512],
                                               psW[par][:], ALU.add), "Vm")
                self.V(_ev)
                evd[dch] = S.val("Vm")

            def _st(sync, qb=qb):
                S.wait(sync, "Vm")
                S.inc(sync.dma_start(out=self.dram[dst][:, qb * 512:(qb + 1) * 512]
                                     .rearrange("(i p) n -> p i n", p=128),
                                     in_=xout[:].rearrange("p (i n) -> p i n", n=512)),
                      "SA", 16)
            self.Sy(_st)

    def ffn(self, src, dst):
        nc, S = self.nc, self.S
        with ExitStack() as outer:
            Eo = outer.enter_context
            rbc, murbc = self._ln_stats(Eo, src, "ff" + dst)
            TC = 1024
            NBLK = 2048
            E = Eo
            xn = [E(nc.sbuf_tensor("fxn%s_%d" % (dst, i), [128, TC], BF16)) for i in range(NDCH)]
            xna = E(nc.sbuf_tensor("fxna" + dst, [128, 512 * 2], F32))
            xnt = E(nc.sbuf_tensor("fxnt" + dst, [128, 512], F32))
            xcols = [E(nc.sbuf_tensor("fxc%s_%d" % (dst, i), [128, TC], F32)) for i in range(NDCH)]
            inter = [E(nc.sbuf_tensor("fint%s_%d" % (dst, i), [128, TC], BF16))
                     for i in range(NBLK // 128)]
            gub = E(nc.sbuf_tensor("fgu" + dst, [128, 2048 * 2 * 2], BF16))
            wdb = E(nc.sbuf_tensor("fwd" + dst, [128, 2048 * 2], BF16))
            g1 = E(nc.sbuf_tensor("fg1" + dst, [128, TC], BF16))
            psG = E(nc.psum_tensor("f_psG" + dst, [128, 512], F32))
            psG2 = E(nc.psum_tensor("f_psG2" + dst, [128, 512], F32))
            psU = E(nc.psum_tensor("f_psU" + dst, [128, 512], F32))
            psU2 = E(nc.psum_tensor("f_psU2" + dst, [128, 512], F32))
            psD = [E(nc.psum_tensor("f_psD%s%d" % (dst, i), [128, 512], F32)) for i in range(4)]

            for tc in range(T // TC):
                prev_dmo = S.val("SB")

                def _lx(sync, tc=tc, prev_dmo=prev_dmo):
                    if tc >= 1:
                        S.wait(sync, "SB", prev_dmo)
                    for dch in range(NDCH):
                        S.inc(sync.dma_start(out=xcols[dch][:],
                                             in_=self.dram[src][dch * 128:(dch + 1) * 128,
                                                                tc * TC:(tc + 1) * TC]),
                              "LA", 16)
                self.Sy(_lx)
                xdma = S.val("LA")

                for dch in range(NDCH):
                    def _xn(vector, tc=tc, dch=dch, xdma=xdma):
                        S.wait(vector, "LA", xdma)
                        gsl = slice(tc * TC, (tc + 1) * TC)
                        vector.tensor_tensor(xn[dch][:], xcols[dch][:], rbc[:, gsl], ALU.mult)
                        S.inc(vector.tensor_tensor(xn[dch][:], xn[dch][:], murbc[:, gsl],
                                                   ALU.subtract), "Vm")
                    self.V(_xn)
                xnready = S.val("Vm")

                last_int = xnready
                for nb in range(MLP // NBLK):
                    intdone = {}
                    for j in range(NBLK // 128):
                        nch = nb * (NBLK // 128) + j
                        par = j % 2
                        gt = gub[:, par * 4096:par * 4096 + 2048]
                        ut = gub[:, par * 4096 + 2048:par * 4096 + 4096]
                        prev_tm = S.val("Tm")

                        lsem = "LB" if par == 0 else "LC"

                        def _lw(sync, nch=nch, gt=gt, ut=ut, j=j, prev_tm=prev_tm, lsem=lsem):
                            S.wait(sync, "Tm", prev_tm)
                            S.inc(sync.dma_start(out=gt.rearrange("p (i n) -> p i n", n=128),
                                                 in_=self.dram["gT"][nch, :, :, :].rearrange("i p n -> p i n")), lsem, 16)
                            S.inc(sync.dma_start(out=ut.rearrange("p (i n) -> p i n", n=128),
                                                 in_=self.dram["uT"][nch, :, :, :].rearrange("i p n -> p i n")), lsem, 16)
                        self.Sy(_lw)
                        wdma = S.val(lsem)

                        def _mm(tensor, gt=gt, ut=ut, wdma=wdma, j=j,
                                nb=nb, tc=tc, lsem=lsem, last_int=last_int):
                            S.wait(tensor, lsem, wdma)
                            S.wait(tensor, "Vm", last_int)
                            for half, bank in ((0, psG), (1, psG2)):
                                for dch in range(NDCH):
                                    tensor.matmul(bank[:], gt[:, dch * 128:(dch + 1) * 128],
                                                  xn[dch][:, half * 512:(half + 1) * 512],
                                                  start=(dch == 0), stop=(dch == NDCH - 1))
                            for half, bank in ((0, psU), (1, psU2)):
                                for dch in range(NDCH):
                                    m = tensor.matmul(bank[:], ut[:, dch * 128:(dch + 1) * 128],
                                                      xn[dch][:, half * 512:(half + 1) * 512],
                                                      start=(dch == 0), stop=(dch == NDCH - 1))
                            S.inc(m, "Tm")
                        self.T_(_mm)
                        tdn = S.val("Tm")

                        def _gelu(scalar, tdn=tdn):
                            S.wait(scalar, "Tm", tdn)
                            scalar.activation(g1[:, 0:512], psG[:], AF.Gelu_apprx_tanh,
                                              bias=0.0, scale=1.0)
                            S.inc(scalar.activation(g1[:, 512:1024], psG2[:], AF.Gelu_apprx_tanh,
                                                    bias=0.0, scale=1.0), "Scm")
                        self.Sc(_gelu)
                        gdone = S.val("Scm")

                        def _int(vector, j=j, gdone=gdone):
                            S.wait(vector, "Scm", gdone)
                            vector.tensor_tensor(inter[j][:, 0:512], g1[:, 0:512], psU[:],
                                                 ALU.mult)
                            S.inc(vector.tensor_tensor(inter[j][:, 512:1024], g1[:, 512:1024],
                                                       psU2[:], ALU.mult), "Vm")
                        self.V(_int)
                        intdone[j] = S.val("Vm")
                        last_int = intdone[j]

                    interdone = S.val("Vm")
                    dwd = {}
                    accd = {}
                    _dtm = [S.val("Tm")]
                    for dch in range(NDCH):
                        par = dch % 2

                        lsem2 = "LD" if par == 0 else "LE"

                        def _lw(sync, nb=nb, dch=dch, par=par, lsem2=lsem2,
                                ptm=None):
                            if dch >= 2:
                                S.wait(sync, "Tm", dwd[dch - 2])
                            else:
                                S.wait(sync, "Tm", S.n["Tm"] if False else _dtm[0])
                            S.inc(sync.dma_start(
                                out=wdb[:, par * 2048:(par + 1) * 2048]
                                    .rearrange("p (i n) -> p i n", n=128),
                                in_=self.dram["dT"][dch, nb * 16:(nb + 1) * 16, :, :].rearrange("i p n -> p i n")),
                                lsem2, 16)
                        self.Sy(_lw)
                        wdma = S.val(lsem2)

                        def _mm(tensor, dch=dch, par=par, wdma=wdma, interdone=interdone, lsem2=lsem2):
                            S.wait(tensor, lsem2, wdma)
                            S.wait(tensor, "Vm", interdone if dch < 2 else accd[dch - 2])
                            for half in range(2):
                                bank = psD[par * 2 + half]
                                for j in range(16):
                                    m = tensor.matmul(bank[:],
                                                      wdb[:, par * 2048 + j * 128:par * 2048 + (j + 1) * 128],
                                                      inter[j][:, half * 512:(half + 1) * 512],
                                                      start=(j == 0), stop=(j == 15))
                            S.inc(m, "Tm")
                        self.T_(_mm)
                        dwd[dch] = S.val("Tm")

                        def _acc(vector, dch=dch, par=par):
                            S.wait(vector, "Tm", dwd[dch])
                            vector.tensor_tensor(xcols[dch][:, 0:512], xcols[dch][:, 0:512],
                                                 psD[par * 2][:], ALU.add)
                            S.inc(vector.tensor_tensor(xcols[dch][:, 512:1024],
                                                       xcols[dch][:, 512:1024],
                                                       psD[par * 2 + 1][:], ALU.add), "Vm")
                        self.V(_acc)
                        accd[dch] = S.val("Vm")

                def _st(sync, tc=tc):
                    S.wait(sync, "Vm")
                    for dch in range(NDCH):
                        S.inc(sync.dma_start(out=self.dram[dst][dch * 128:(dch + 1) * 128,
                                                                tc * TC:(tc + 1) * TC],
                                             in_=xcols[dch][:]), "SB", 16)
                self.Sy(_st)
            self.barrier()


_BUILT = {}


def _get_nc(dev_stop=None):
    if dev_stop not in _BUILT:
        _BUILT[dev_stop] = Builder(dev_stop).build()
    return _BUILT[dev_stop]


def _shard_inputs(inputs):
    x = np.asarray(inputs["L0_attn_a_x"])
    base = {
        "aWq": np.ascontiguousarray(inputs["L0_attn_a_Wq"], dtype=np.float32),
        "aWk": np.ascontiguousarray(inputs["L0_attn_a_Wk"], dtype=np.float32),
        "aWv": np.ascontiguousarray(inputs["L0_attn_a_Wv"], dtype=np.float32),
        "aWg": np.ascontiguousarray(inputs["L0_attn_a_Wg"], dtype=np.float32),
        "aWo": np.ascontiguousarray(inputs["L0_attn_a_Wo"], dtype=np.float32),
        "bWq": np.ascontiguousarray(
            np.asarray(inputs["L0_attn_b_Wq"], dtype=np.float32).transpose(1, 0, 2).reshape(D, H * HD)),
        "bbq": np.ascontiguousarray(np.asarray(inputs["L0_attn_b_bq"], dtype=np.float32).reshape(H, HD).T),
        "bWk": np.ascontiguousarray(
            np.asarray(inputs["L0_attn_b_Wk"], dtype=np.float32).transpose(1, 0, 2).reshape(D, G * HD)),
        "bbk": np.ascontiguousarray(np.asarray(inputs["L0_attn_b_bk"], dtype=np.float32).reshape(G, HD).T),
        "bWv": np.ascontiguousarray(
            np.asarray(inputs["L0_attn_b_Wv"], dtype=np.float32).transpose(1, 0, 2).reshape(D, G * HD)),
        "bbv": np.ascontiguousarray(np.asarray(inputs["L0_attn_b_bv"], dtype=np.float32).reshape(G, HD).T),
        "gw": np.ascontiguousarray(inputs["L0_ffn_gate_W_packed"], dtype=np.int32),
        "gs": np.ascontiguousarray(inputs["L0_ffn_gate_scales"], dtype=np.float32),
        "uw": np.ascontiguousarray(inputs["L0_ffn_up_W_packed"], dtype=np.int32),
        "us": np.ascontiguousarray(inputs["L0_ffn_up_scales"], dtype=np.float32),
        "dw": np.ascontiguousarray(inputs["L0_ffn_down_W_packed"], dtype=np.int32),
        "ds": np.ascontiguousarray(inputs["L0_ffn_down_scales"], dtype=np.float32),
    }
    maps = []
    for b in range(8):
        m = dict(base)
        m["x0"] = np.ascontiguousarray(np.asarray(x[b], dtype=np.float32).T)
        maps.append(m)
    return maps


def kernel(**inputs):
    nc = _get_nc(None)
    maps = _shard_inputs(inputs)
    res = run_bass_kernel_spmd(nc, maps, core_ids=list(range(8)))
    return np.stack([np.asarray(r["out"], dtype=np.float32).T for r in res.results], axis=0)
